# version 33
# baseline (speedup 1.0000x reference)
import numpy as np
from contextlib import ExitStack

import concourse.bass as bass
import concourse.mybir as mybir
from concourse.bass_utils import run_bass_kernel_spmd

F32 = mybir.dt.float32
BF16 = mybir.dt.bfloat16
I32 = mybir.dt.int32
AF = mybir.ActivationFunctionType
AL = mybir.AluOpType
AX = mybir.AxisListType.X

NT, MY, D, H = 4096, 2048, 1024, 16
KC = 8
NKT = 33
EPS = 1e-5
DQ = 1.0 / 127.0

ENGINES = ["sync", "scalar", "tensor", "vector", "gpsimd"]
SEM_NAMES = [
    "mm", "ac", "ev",
    "dxa", "dxb", "dst", "dtl", "dwv", "dwq", "dwk", "dmsc",
    "dsp0", "dsp1", "dwd", "dgw", "dgw1", "ddn",
    "dpk0", "dpk1", "dgl0", "dgl1", "dut0", "dut1", "dht0", "dht1",
    "dhl0", "dhl1", "dxc0", "dxc1", "dtm0", "dtm1", "dou0", "dou1",
]

_CACHE = {}


class Hp:

    def __init__(self, me, eng, ct, st, sems):
        self.me, self.eng, self.ct, self.st, self.sems = me, eng, ct, st, sems

    def add(self, k, by=1):
        self.ct[k] = self.ct.get(k, 0) + by
        return self.ct[k]

    def cnt(self, k):
        return self.ct.get(k, 0)

    def W(self, owner, sem, val=None):
        if self.me == owner:
            self.eng.wait_ge(self.sems[sem], self.cnt(sem) if val is None else val)

    def dma(self, q, out, in_, sem, **kw):
        if self.me == q:
            self.eng.dma_start(out=out, in_=in_, **kw).then_inc(self.sems[sem], 16)
        self.add(sem, 16)

    def mm(self, out, lhsT, rhs, start=True, stop=True, tp=None, inc=True):
        if self.me == "tensor":
            i = self.eng.matmul(out, lhsT, rhs, start=start, stop=stop,
                                tile_position=tp, skip_group_check=True)
            if inc:
                i.then_inc(self.sems["mm"], 1)
        if inc:
            self.add("mm")

    def tr(self, out, in_, ident, inc=True):
        if self.me == "tensor":
            i = self.eng.transpose(out, in_, ident)
            if inc:
                i.then_inc(self.sems["mm"], 1)
        if inc:
            self.add("mm")

    def act(self, out, in_, func, bias=0.0, scale=1.0, inc=True):
        if self.me == "scalar":
            i = self.eng.activation(out, in_, func, bias=bias, scale=scale)
            if inc:
                i.then_inc(self.sems["ac"], 1)
        if inc:
            self.add("ac")

    def V(self, method, *args, inc=True, **kw):
        if self.me == "vector":
            i = getattr(self.eng, method)(*args, **kw)
            if inc:
                i.then_inc(self.sems["ev"], 1)
        if inc:
            self.add("ev")

    def barrier(self, dma_sems=()):
        for s in ("mm", "ac", "ev"):
            own = {"mm": "tensor", "ac": "scalar", "ev": "vector"}[s]
            if self.me != own:
                self.W(self.me, s)
        for s in dma_sems:
            self.W(self.me, s)


def _build():
    nc = bass.Bass("TRN2", target_bir_lowering=False,
                   detect_race_conditions=False)
    dram = {}

    def din(name, shape, dt):
        dram[name] = nc.dram_tensor(name, shape, dt, kind="ExternalInput").ap()

    din("x", [NT, D], F32)
    din("wq", [D, 512], F32); din("wqc", [D, 512], F32)
    din("wk", [D, 512], F32); din("wkc", [D, 512], F32)
    din("wv", [D, D], F32); din("wvc", [D, D], F32)
    din("bq4", [128, 4], F32); din("bqc4", [128, 4], F32)
    din("bk4", [128, 4], F32); din("bkc4", [128, 4], F32)
    din("bvr", [1, D], F32); din("bvcr", [1, D], F32)
    din("wg", [24 * 1024, 128], I32)
    din("wu", [24 * 1024, 128], I32)
    din("wd", [3072, 1024], I32)
    din("ident", [16, 16], F32)
    din("bo4", [128, 4], F32)
    y = nc.dram_tensor("y", [MY, D], F32, kind="ExternalOutput").ap()
    xn_hbm = nc.dram_tensor("xn_hbm", [NT, D], BF16, kind="Internal").ap()
    s_hbm = nc.dram_tensor("s_hbm", [D, MY], BF16, kind="Internal").ap()
    dn_hbm = nc.dram_tensor("dn_hbm", [16, MY], F32, kind="Internal").ap()
    gr_hbm = nc.dram_tensor("gr_hbm", [3072, MY], BF16, kind="Internal").ap()
    ur_hbm = nc.dram_tensor("ur_hbm", [3072, MY], BF16, kind="Internal").ap()
    h_hbm = nc.dram_tensor("h_hbm", [3072, MY], BF16, kind="Internal").ap()

    cts = {e: {} for e in ENGINES}
    sts = {e: {} for e in ENGINES}

    with ExitStack() as top:
        def sb(stk, name, shape, dt):
            return stk.enter_context(nc.sbuf_tensor(name, shape, dt))

        ones1 = sb(top, "ones1", [1, 128], BF16)
        ones16 = sb(top, "ones16", [128, 16], BF16)
        blockones = sb(top, "blockones", [128, 4], BF16)
        idb = sb(top, "idb", [16, 16], F32)
        eps_t = sb(top, "eps_t", [128, 1], F32)
        bq4 = sb(top, "sbq4", [128, 4], F32); bqc4 = sb(top, "sbqc4", [128, 4], F32)
        bk4 = sb(top, "sbk4", [128, 4], F32); bkc4 = sb(top, "sbkc4", [128, 4], F32)
        bvr_b = sb(top, "bvr_b", [1, D], BF16)
        bvcr_b = sb(top, "bvcr_b", [1, D], BF16)
        normbuf = sb(top, "normbuf", [4, 64], F32)
        nb_row = sb(top, "nb_row", [1, 200], F32)
        m_sc = sb(top, "m_sc", [1, 4], F32)
        mneg_b = sb(top, "mneg_b", [1, 1], BF16)
        bias_mneg = sb(top, "bias_mneg", [128, 1], F32)
        kp4 = sb(top, "kp4", [128, 4], F32)
        sig4 = sb(top, "sig4", [128, 4], F32)
        kp4b = sb(top, "kp4b", [128, 4], BF16)
        kp_t = [sb(top, f"kp_t{i}", [128, 128], BF16) for i in range(4)]
        dpiece = sb(top, "dpiece", [1, 1024], F32)
        xn_my = [sb(top, f"xn_my{c}", [128, MY], BF16) for c in range(KC)]

        ps_sc = top.enter_context(nc.psum_tensor("ps_sc", [128, 2048], F32))
        ps_cx = top.enter_context(nc.psum_tensor("ps_cx", [128, 2048], F32))
        sems = {n: top.enter_context(nc.semaphore(n)) for n in SEM_NAMES}

        def phase(fn):
            with nc.Block() as block:
                for ename in ENGINES:
                    def body(eng, ename=ename):
                        fn(Hp(ename, eng, cts[ename], sts[ename], sems))
                    getattr(block, ename)(body)

        with nc.Block() as b0:
            @b0.sync
            def _(sync):
                for n in SEM_NAMES:
                    sync.sem_clear(sems[n])

        with ExitStack() as stVQK:
            v_aug = [sb(stVQK, f"v_aug{t}", [128, 1040], BF16) for t in range(NKT)]
            q_sb = [sb(stVQK, f"q_sb{c}", [128, MY], BF16) for c in range(4)]
            k_pad = [sb(stVQK, f"k_pad{c}", [128, NT], BF16) for c in range(4)]

            with ExitStack() as stXN:
                xn_pr = [sb(stXN, f"xn_pr{c}", [128, MY], BF16) for c in range(KC)]

                def xn_chunk(c, col0, n):
                    if col0 < MY:
                        return xn_my[c][:, col0:col0 + n]
                    return xn_pr[c][:, col0 - MY:col0 - MY + n]

                with ExitStack() as stA:
                    xt = [sb(stA, f"xt{j}", [128, D], F32) for j in range(2)]
                    xnt = sb(stA, "xnt", [128, D], BF16)
                    xsq = sb(stA, "xsq", [128, D], F32)
                    ln = [sb(stA, f"ln{i}", [128, 32], F32) for i in range(4)]
                    (ln_s1, ln_s2, ln_mu, ln_rs) = ln

                    def phaseA(h):
                        h.V("memset", xsq[:], 1.0)
                        h.W("scalar", "ev")
                        for k in range(3):
                            h.act(xsq[:, 256 * k:256 * (k + 1)],
                                  xsq[:, 256 * k:256 * (k + 1)], AF.Sqrt)
                        h.V("memset", ones1[:], 1.0)
                        h.V("memset", eps_t[:], EPS)
                        h.V("memset", ones16[:], 1.0)
                        h.dma("gpsimd", blockones[:], dram["bo4"], "dmsc")
                        for nm, dst in [("bq4", bq4), ("bqc4", bqc4),
                                        ("bk4", bk4), ("bkc4", bkc4),
                                        ("ident", idb)]:
                            h.dma("gpsimd", dst[:], dram[nm], "dmsc")
                        h.dma("gpsimd", bvr_b[:], dram["bvr"], "dmsc")
                        h.dma("gpsimd", bvcr_b[:], dram["bvcr"], "dmsc")
                        xt_free = {0: 0, 1: 0}
                        ssem = ["dxa", "dxb"]
                        for tt in range(NT // 128):
                            j = tt % 2
                            if tt >= 2:
                                h.W("sync", "ev", xt_free[j])
                            h.dma("sync", xt[j][:],
                                  dram["x"][128 * tt:128 * (tt + 1), :], ssem[j])
                            h.W("vector", ssem[j])
                            h.V("reduce_sum", ln_s1[:, tt:tt + 1], xt[j][:],
                                axis=AX)
                            h.V("tensor_mul", xsq[:], xt[j][:], xt[j][:])
                            h.V("reduce_sum", ln_s2[:, tt:tt + 1], xsq[:],
                                axis=AX)
                            xt_free[j] = h.cnt("ev")
                        h.V("tensor_scalar_mul", ln_mu[:], ln_s1[:], 1.0 / D)
                        h.V("tensor_scalar_mul", ln_s1[:], ln_s2[:], 1.0 / D)
                        h.V("tensor_mul", ln_s2[:], ln_mu[:], ln_mu[:])
                        h.V("tensor_sub", ln_s1[:], ln_s1[:], ln_s2[:])
                        h.W("scalar", "ev")
                        h.act(ln_s2[:], ln_s1[:], AF.Sqrt, bias=eps_t[:, 0:1])
                        h.W("vector", "ac")
                        h.V("reciprocal", ln_rs[:], ln_s2[:])
                        for tt in range(NT // 128):
                            j = tt % 2
                            h.W("sync", "ev", xt_free[j])
                            h.dma("sync", xt[j][:],
                                  dram["x"][128 * tt:128 * (tt + 1), :], ssem[j])
                            h.W("vector", ssem[j])
                            h.W("vector", "dst", 16 * tt)
                            h.V("tensor_scalar", xnt[:], xt[j][:],
                                ln_mu[:, tt:tt + 1], ln_rs[:, tt:tt + 1],
                                AL.subtract, AL.mult)
                            xt_free[j] = h.cnt("ev")
                            h.W("sync", "ev")
                            h.dma("sync", xn_hbm[128 * tt:128 * (tt + 1), :],
                                  xnt[:], "dst")
                            if tt == 15:
                                h.W("sync", "dst", 16 * 16)
                                for c in range(KC):
                                    h.dma("sync", xn_my[c][:],
                                          xn_hbm[0:MY, 128 * c:128 * (c + 1)],
                                          "dtl", transpose=True)
                        h.W("scalar", "ev")
                        h.act(xsq[:, 768:1024], xsq[:, 768:1024], AF.Sigmoid)
                        h.W("sync", "dst")
                        for c in range(KC):
                            h.dma("sync", xn_pr[c][:],
                                  xn_hbm[MY:NT, 128 * c:128 * (c + 1)], "dtl",
                                  transpose=True)
                        h.barrier(dma_sems=["dst", "dmsc"])

                    phase(phaseA)

                with ExitStack() as stB:
                    sigb = [sb(stB, f"sigb{j}", [128, 512], F32) for j in range(2)]

                    with ExitStack() as stWV:
                        wvb = sb(stWV, "wvb", [128, 2048], BF16)
                        wvcb = sb(stWV, "wvcb", [128, 2048], BF16)

                        def phaseB3(h):
                            h.st.setdefault("pfree", {0: 0, 1: 0})
                            h.st.setdefault("sfree", {0: 0, 1: 0})
                            h.st.setdefault("w_uses", 0)
                            for t in range(32):
                                h.V("memset",
                                    bass.AP(v_aug[t], 64, [[1040, 128], [65, 16]]),
                                    1.0, inc=(t == 31))
                            for qt in range(4):
                                h.W("gpsimd", "mm", h.st["w_uses"])
                                h.dma("gpsimd", wvb[:],
                                      bass.AP(dram["wv"].tensor, 256 * qt,
                                              [[1024, 128], [128 * 1024, 8],
                                               [1, 256]]), "dwv")
                                h.dma("gpsimd", wvcb[:],
                                      bass.AP(dram["wvc"].tensor, 256 * qt,
                                              [[1024, 128], [128 * 1024, 8],
                                               [1, 256]]), "dwv")
                                h.W("tensor", "dwv")
                                if qt == 0:
                                    h.W("tensor", "dtl", 16 * KC)
                                for tt in list(range(16)) + list(range(16, 32)):
                                    if qt == 0 and tt == 16:
                                        h.W("tensor", "dtl", 32 * KC)
                                    p = tt % 2
                                    raw = bass.AP(ps_sc, 1024 * p,
                                                  [[2048, 128], [1, 256]])
                                    gate = bass.AP(ps_sc, 1024 * p + 512,
                                                   [[2048, 128], [1, 256]])
                                    h.W("tensor", "ev", h.st["pfree"][p])
                                    h.mm(raw, ones1[0:1, 0:128],
                                         bvr_b[0:1, 256 * qt:256 * (qt + 1)],
                                         start=True, stop=False, inc=False)
                                    h.mm(gate, ones1[0:1, 0:128],
                                         bvcr_b[0:1, 256 * qt:256 * (qt + 1)],
                                         start=True, stop=False, inc=False)
                                    for kc in range(KC):
                                        xnc = xn_chunk(kc, 128 * tt, 128)
                                        h.mm(raw, xnc,
                                             wvb[:, 256 * kc:256 * (kc + 1)],
                                             start=False, stop=(kc == KC - 1),
                                             inc=False)
                                    for kc in range(KC):
                                        xnc = xn_chunk(kc, 128 * tt, 128)
                                        h.mm(gate, xnc,
                                             wvcb[:, 256 * kc:256 * (kc + 1)],
                                             start=False, stop=(kc == KC - 1),
                                             inc=(kc == KC - 1))
                                    h.st["w_uses"] = h.cnt("mm")
                                    h.W("scalar", "mm")
                                    h.W("scalar", "ev", h.st["sfree"][p])
                                    h.act(sigb[p][:, 0:256], gate, AF.Sigmoid)
                                    h.W("vector", "ac")
                                    h.V("tensor_mul",
                                        bass.AP(v_aug[tt], 260 * qt,
                                                [[1040, 128], [65, 4], [1, 64]]),
                                        bass.AP(ps_sc, 1024 * p,
                                                [[2048, 128], [64, 4], [1, 64]]),
                                        bass.AP(sigb[p], 0,
                                                [[512, 128], [64, 4], [1, 64]]))
                                    h.st["pfree"][p] = h.cnt("ev")
                                    h.st["sfree"][p] = h.cnt("ev")
                            h.barrier(dma_sems=["dwv"])

                        phase(phaseB3)

                    with ExitStack() as stN:
                        nsq = sb(stN, "nsq", [128, MY], BF16)

                        def mk_proj(dst, wname, wcname, b_ap, bc_ap, ntiles,
                                    norm_col0, slot_sem, wqb, wqcb):
                            def run(h):
                                pfree, sfree = h.st["pfree"], h.st["sfree"]
                                for cc in range(4):
                                    h.W("gpsimd", "mm", h.st["w_uses"])
                                    h.dma("gpsimd", wqb[:],
                                          bass.AP(dram[wname].tensor, 128 * cc,
                                                  [[512, 128], [128 * 512, 8],
                                                   [1, 128]]), slot_sem)
                                    h.dma("gpsimd", wqcb[:],
                                          bass.AP(dram[wcname].tensor, 128 * cc,
                                                  [[512, 128], [128 * 512, 8],
                                                   [1, 128]]), slot_sem)
                                    h.W("tensor", slot_sem)
                                    for nt in range(ntiles):
                                        p = nt % 2
                                        raw = bass.AP(ps_sc, 1024 * p,
                                                      [[2048, 128], [1, 512]])
                                        gate = bass.AP(ps_sc, 1024 * p + 512,
                                                       [[2048, 128], [1, 512]])
                                        h.W("tensor", "ev", pfree[p])
                                        for kc in range(KC):
                                            xnc = xn_chunk(kc, 512 * nt, 512)
                                            h.mm(raw,
                                                 wqb[:, 128 * kc:128 * (kc + 1)],
                                                 xnc, start=(kc == 0),
                                                 stop=(kc == KC - 1), inc=False)
                                        for kc in range(KC):
                                            xnc = xn_chunk(kc, 512 * nt, 512)
                                            h.mm(gate,
                                                 wqcb[:, 128 * kc:128 * (kc + 1)],
                                                 xnc, start=(kc == 0),
                                                 stop=(kc == KC - 1),
                                                 inc=(kc == KC - 1))
                                        h.st["w_uses"] = h.cnt("mm")
                                        h.W("scalar", "mm")
                                        h.W("scalar", "ev", sfree[p])
                                        h.act(sigb[p][:, 0:512], gate, AF.Sigmoid,
                                              bias=bc_ap[:, cc:cc + 1])
                                        h.W("vector", "ac")
                                        h.V("scalar_tensor_tensor",
                                            dst[cc][:, 512 * nt:512 * (nt + 1)],
                                            raw, b_ap[:, cc:cc + 1],
                                            sigb[p][:, 0:512], AL.add, AL.mult)
                                        pfree[p] = h.cnt("ev")
                                        sfree[p] = h.cnt("ev")
                                    for w0 in range(0, ntiles, 4):
                                        wn = min(4, ntiles - w0)
                                        h.V("tensor_mul", nsq[:, 0:512 * wn],
                                            dst[cc][:, 512 * w0:512 * (w0 + wn)],
                                            dst[cc][:, 512 * w0:512 * (w0 + wn)])
                                        h.W("tensor", "ev")
                                        for k2 in range(wn):
                                            pn = k2 % 2
                                            h.W("tensor", "ev", pfree[pn])
                                            h.mm(bass.AP(ps_sc, 1024 * pn,
                                                         [[2048, 4], [1, 512]]),
                                                 blockones[:, 0:4],
                                                 nsq[:, 512 * k2:512 * (k2 + 1)],
                                                 start=True, stop=True)
                                            h.W("vector", "mm")
                                            idx = norm_col0 + ntiles * cc + w0 + k2
                                            h.V("reduce_max",
                                                normbuf[0:4, idx:idx + 1],
                                                bass.AP(ps_sc, 1024 * pn,
                                                        [[2048, 4], [1, 512]]),
                                                axis=AX)
                                            pfree[pn] = h.cnt("ev")
                                h.barrier(dma_sems=[slot_sem])
                            return run

                        with ExitStack() as stWQ:
                            wqb = sb(stWQ, "wqb", [128, 1024], BF16)
                            wqcb = sb(stWQ, "wqcb", [128, 1024], BF16)
                            phase(mk_proj(q_sb, "wq", "wqc", bq4, bqc4, MY // 512,
                                          0, "dwq", wqb, wqcb))
                        with ExitStack() as stWK:
                            wkb = sb(stWK, "wkb", [128, 1024], BF16)
                            wkcb = sb(stWK, "wkcb", [128, 1024], BF16)
                            phase(mk_proj(k_pad, "wk", "wkc", bk4, bkc4, NT // 512,
                                          16, "dwk", wkb, wkcb))

            stB4 = ExitStack()
            vp_row = sb(stB4, "vp_row", [1, D], BF16)
            var_row = sb(stB4, "var_row", [1, 1040], BF16)

            def phaseB4(h):
                h.act(sig4[:], bkc4[:], AF.Sigmoid)
                h.act(vp_row[:], bvcr_b[:], AF.Sigmoid)
                h.W("vector", "ac")
                h.V("tensor_mul", kp4[:], bk4[:], sig4[:])
                h.V("tensor_mul", vp_row[:], bvr_b[:], vp_row[:])
                for qd in range(4):
                    h.V("memset", kp_t[qd][:], 0.0)
                    h.V("tensor_scalar_mul", kp_t[qd][:, 0:16], ones16[:],
                        kp4[:, qd:qd + 1])
                h.V("memset", v_aug[32][:], 0.0)
                h.V("memset", var_row[:], 0.0)
                h.V("tensor_copy",
                    bass.AP(var_row, 0, [[1040, 1], [65, 16], [1, 64]]),
                    bass.AP(vp_row, 0, [[1024, 1], [64, 16], [1, 64]]))
                h.V("memset", bass.AP(var_row, 64, [[1040, 1], [65, 16]]), 1.0)
                h.W("tensor", "ev")
                for i, (c0, n) in enumerate([(0, 512), (512, 512), (1024, 16)]):
                    h.mm(bass.AP(ps_cx, c0, [[2048, 16], [1, n]]),
                         ones1[0:1, 0:16], var_row[0:1, c0:c0 + n],
                         start=True, stop=True, inc=(i == 2))
                h.W("vector", "mm")
                h.V("tensor_copy", v_aug[32][0:16, :],
                    bass.AP(ps_cx, 0, [[2048, 16], [1, 1040]]))
                h.V("tensor_mul", kp4b[:], kp4[:], kp4[:])
                h.W("tensor", "ev")
                h.mm(bass.AP(ps_cx, 0, [[2048, 4], [1, 4]]),
                     blockones[:, 0:4], kp4b[:, 0:4], start=True, stop=True)
                h.W("vector", "mm")
                h.V("reduce_max", normbuf[0:4, 48:49],
                    bass.AP(ps_cx, 0, [[2048, 4], [1, 4]]), axis=AX)
                h.W("gpsimd", "ev")
                h.dma("gpsimd", nb_row[0:1, 0:196], normbuf[0:4, 0:49], "dmsc")
                h.W("vector", "dmsc")
                h.V("reduce_max", m_sc[0:1, 0:1], nb_row[0:1, 0:196], axis=AX)
                h.V("tensor_scalar_mul", mneg_b[:], m_sc[0:1, 0:1], -0.1275)
                h.W("tensor", "ev")
                h.mm(bass.AP(ps_cx, 0, [[2048, 128], [1, 1]]),
                     ones1[0:1, 0:128], mneg_b[0:1, 0:1],
                     start=True, stop=True)
                h.W("vector", "mm")
                h.V("tensor_copy", bias_mneg[:],
                    bass.AP(ps_cx, 0, [[2048, 128], [1, 1]]))
                h.act(sig4[:], sig4[:], AF.Exp, scale=0.0)
                h.barrier(dma_sems=["dmsc"])

            phase(phaseB4)
            stB4.close()

            with ExitStack() as stC:
                p_sb = [sb(stC, f"p_sb{j}", [128, 1024], BF16) for j in range(3)]
                s_piece = [sb(stC, f"s_piece{j}", [128, 512], BF16)
                           for j in range(2)]
                ptile = [sb(stC, f"ptile{j}", [128, 2048], BF16)
                         for j in range(2)]
                wgb = [sb(stC, f"wgb{j}", [128, 2048], BF16) for j in range(2)]
                wub = [sb(stC, f"wub{j}", [128, 2048], BF16) for j in range(2)]

                def phaseC(h):
                    h.V("memset", p_sb[0][:], 0.0)
                    h.W("scalar", "ev")
                    for k in range(2):
                        h.act(p_sb[0][:, 512 * k:512 * (k + 1)],
                              p_sb[0][:, 512 * k:512 * (k + 1)], AF.Exp,
                              scale=0.0)

                    wg_uses = {0: 0, 1: 0}
                    wsem = {0: "dgw", 1: "dgw1"}
                    pk_sem = {0: "dpk0", 1: "dpk1"}
                    pk_uses = {0: 0, 1: 0}
                    ffn_free = {0: 0, 1: 0}

                    def load_super(sbi):
                        if sbi >= 12:
                            return
                        wj = sbi % 2
                        h.W("gpsimd", "mm", wg_uses[wj])
                        off = 2 * 1024 * 128 * sbi
                        h.dma("gpsimd", wgb[wj][:],
                              bass.AP(dram["wg"].tensor, off,
                                      [[128, 128], [1024 * 128, 2],
                                       [128 * 128, 8], [1, 128]]), wsem[wj])
                        h.dma("gpsimd", wub[wj][:],
                              bass.AP(dram["wu"].tensor, off,
                                      [[128, 128], [1024 * 128, 2],
                                       [128 * 128, 8], [1, 128]]), wsem[wj])

                    load_super(0)
                    load_super(1)

                    def emit_burst(b):
                        hb = b // 8
                        sub = b % 8
                        kind = sub // 4
                        nt = sub % 4
                        sbi = hb // 2
                        wj = sbi % 2
                        if b == 16 * sbi:
                            h.W("tensor", wsem[wj])
                        bank = b % 2
                        ps = bass.AP(ps_cx, 1024 + 512 * bank,
                                     [[2048, 128], [1, 512]])
                        h.W("tensor", "ev", ffn_free[bank])
                        wt = wgb[wj] if kind == 0 else wub[wj]
                        for kc in range(KC):
                            h.mm(ps, wt[:, 1024 * (hb % 2) + 128 * kc:
                                        1024 * (hb % 2) + 128 * (kc + 1)],
                                 xn_my[kc][:, 512 * nt:512 * (nt + 1)],
                                 start=(kc == 0), stop=(kc == KC - 1),
                                 inc=(kc == KC - 1))
                        if b == 16 * sbi + 15:
                            wg_uses[wj] = h.cnt("mm")
                            load_super(sbi + 2)
                        pslot = (b // 4) % 2
                        h.W("vector", "mm")
                        if nt == 0:
                            h.W("vector", pk_sem[pslot], 16 * pk_uses[pslot])
                        h.V("tensor_copy",
                            ptile[pslot][:, 512 * nt:512 * (nt + 1)], ps)
                        ffn_free[bank] = h.cnt("ev")
                        if nt == 3:
                            h.W("gpsimd", "ev")
                            tgt = gr_hbm if kind == 0 else ur_hbm
                            h.dma("gpsimd", tgt[128 * hb:128 * (hb + 1), :],
                                  ptile[pslot][:], pk_sem[pslot])
                            pk_uses[pslot] += 1

                    FILL_KTS = (1, 2, 3, 16, 23, 30)
                    burst = {"b": 0}
                    ev_ctx_free = h.cnt("ev")
                    slot_last = {0: 0, 1: 0}
                    ac_of = {}
                    unit = 0
                    for qd in range(4):
                        for pj in range(2):
                            jA, jB = 2 * pj, 2 * pj + 1
                            hA, hB = 4 * qd + jA, 4 * qd + jB
                            for qc in range(4):
                                pc = unit % 2
                                psem = "dsp0" if pc == 0 else "dsp1"
                                for kt in range(NKT + 2):
                                    par = kt % 2
                                    if kt < NKT:
                                        h.W("tensor", "ac", slot_last[par])
                                        for hx, j in ((0, jA), (1, jB)):
                                            if kt < 32:
                                                lhsT = k_pad[qd][
                                                    32 * j:32 * j + 16,
                                                    128 * kt:128 * (kt + 1)]
                                            else:
                                                lhsT = kp_t[qd][
                                                    32 * j:32 * j + 16, :]
                                            h.mm(
                                                ps_sc[:, 1024 * par + 512 * hx:
                                                      1024 * par +
                                                      512 * (hx + 1)],
                                                lhsT,
                                                q_sb[qd][32 * j:32 * j + 16,
                                                         512 * qc:
                                                         512 * (qc + 1)],
                                                start=True, stop=True,
                                                tp=(32 * j, 0), inc=(hx == 1))
                                        h.W("scalar", "mm")
                                        h.act(
                                            bass.AP(p_sb[kt % 3], 0,
                                                    [[1024, 128], [1, 1024]]),
                                            bass.AP(ps_sc, 1024 * par,
                                                    [[2048, 128], [1, 1024]]),
                                            AF.Exp, bias=bias_mneg[:, 0:1],
                                            scale=0.125)
                                        slot_last[par] = h.cnt("ac")
                                        ac_of[kt] = h.cnt("ac")
                                    if kt > 1:
                                        ks = kt - 2
                                        h.W("tensor", "ac", ac_of[ks])
                                        if ks == 0:
                                            h.W("tensor", "ev", ev_ctx_free)
                                        for hx, hh in ((0, hA), (1, hB)):
                                            h.mm(
                                                ps_cx[0:65,
                                                      512 * hx:512 * (hx + 1)],
                                                v_aug[ks][:, 65 * hh:
                                                          65 * hh + 65],
                                                p_sb[ks % 3][:,
                                                             512 * hx:
                                                             512 * (hx + 1)],
                                                start=(ks == 0),
                                                stop=(ks == 32),
                                                inc=(hx == 1))
                                    if kt in FILL_KTS and burst["b"] < 192:
                                        emit_burst(burst["b"])
                                        burst["b"] += 1
                                h.W("vector", "mm")
                                h.W("vector", psem, 16 * (unit // 2))
                                h.W("vector", "ddn", 16 * unit)
                                for hx in range(2):
                                    h.V("tensor_copy",
                                        s_piece[pc][64 * hx:64 * (hx + 1), :],
                                        ps_cx[0:64, 512 * hx:512 * (hx + 1)],
                                        inc=False)
                                h.V("tensor_copy", dpiece[0:1, 0:1024],
                                    ps_cx[64:65, 0:1024])
                                ev_ctx_free = h.cnt("ev")
                                h.W("sync", "ev")
                                h.dma("sync",
                                      s_hbm[128 * (2 * qd + pj):
                                            128 * (2 * qd + pj + 1),
                                            512 * qc:512 * (qc + 1)],
                                      s_piece[pc][:], psem)
                                h.dma("sync",
                                      dn_hbm[hA:hA + 2,
                                             512 * qc:512 * (qc + 1)],
                                      dpiece[0:1, 0:1024], "ddn")
                                unit += 1
                    while burst["b"] < 192:
                        emit_burst(burst["b"])
                        burst["b"] += 1
                    h.barrier(dma_sems=["dsp0", "dsp1", "ddn", "dpk0", "dpk1",
                                        "dgw", "dgw1"])

                phase(phaseC)

        with ExitStack() as stT:
            rec_tm = sb(stT, "rec_tm", [128, 256], F32)
            h_sb = [sb(stT, f"h_sb{i}", [128, MY], BF16) for i in range(24)]

            with ExitStack() as stT0:
                denom_sb = sb(stT0, "denom_sb", [16, MY], F32)
                recip = sb(stT0, "recip", [16, MY], F32)

                def phaseT0(h):
                    h.V("memset", rec_tm[0:16, 0:64], 0.0)
                    h.W("scalar", "ev")
                    for k in range(2):
                        h.act(rec_tm[0:16, 32 * k:32 * (k + 1)],
                              rec_tm[0:16, 32 * k:32 * (k + 1)],
                              AF.Silu, scale=0.0)
                    h.dma("gpsimd", denom_sb[:], dn_hbm, "dmsc")
                    h.W("vector", "dmsc")
                    h.V("reciprocal", recip[:], denom_sb[:])
                    h.W("tensor", "ev")
                    h.W("tensor", "ac")
                    tr_free = {0: 0, 1: 0}
                    for tt in range(16):
                        p = tt % 2
                        h.W("tensor", "ev", tr_free[p])
                        h.tr(bass.AP(ps_sc, 1024 * p, [[2048, 128], [1, 16]]),
                             recip[0:16, 128 * tt:128 * (tt + 1)], idb[:])
                        h.W("vector", "mm")
                        h.V("tensor_copy", rec_tm[:, 16 * tt:16 * (tt + 1)],
                            bass.AP(ps_sc, 1024 * p, [[2048, 128], [1, 16]]))
                        tr_free[p] = h.cnt("ev")
                    h.barrier(dma_sems=["dmsc"])

                phase(phaseT0)

            with ExitStack() as stT1:
                gtile = [sb(stT1, f"gtile{j}", [128, 2048], BF16)
                         for j in range(2)]
                utile = [sb(stT1, f"utile{j}", [128, 2048], BF16)
                         for j in range(2)]
                silu_t = [sb(stT1, f"silu_t{j}", [128, 2048], BF16)
                          for j in range(2)]

                def phaseT1(h):
                    gl_sem = {0: "dgl0", 1: "dgl1"}
                    ut_sem = {0: "dut0", 1: "dut1"}
                    sl_ac = {0: 0, 1: 0}
                    sl_ev = {0: 0, 1: 0}
                    for hb in range(24):
                        sl = hb % 2
                        if hb >= 2:
                            h.W("gpsimd", "ac", sl_ac[sl])
                            h.W("sync", "ev", sl_ev[sl])
                        h.dma("gpsimd", gtile[sl][:],
                              gr_hbm[128 * hb:128 * (hb + 1), :], gl_sem[sl])
                        h.dma("sync", utile[sl][:],
                              ur_hbm[128 * hb:128 * (hb + 1), :], ut_sem[sl])
                        h.W("scalar", gl_sem[sl])
                        h.W("scalar", "ev", sl_ev[sl])
                        h.act(silu_t[sl][:], gtile[sl][:], AF.Silu, scale=DQ)
                        sl_ac[sl] = h.cnt("ac")
                        h.W("vector", "ac")
                        h.W("vector", ut_sem[sl])
                        h.V("scalar_tensor_tensor", h_sb[hb][:], utile[sl][:],
                            DQ, silu_t[sl][:], AL.mult, AL.mult)
                        sl_ev[sl] = h.cnt("ev")
                    h.barrier()

                phase(phaseT1)

            with ExitStack() as stT2:
                wd_sb = [sb(stT2, f"wd_sb{i}", [128, 2048], BF16)
                         for i in range(6)]
                g_all = [sb(stT2, f"g_all{i}", [128, 512], BF16)
                         for i in range(16)]
                g_b = [sb(stT2, f"g_b{j}", [128, 512], F32) for j in range(2)]
                x_cmb = [sb(stT2, f"x_cmb{j}", [128, D], F32) for j in range(2)]
                s_tm = [sb(stT2, f"s_tm{j}", [128, D], BF16) for j in range(2)]
                out_b = [sb(stT2, f"out_b{j}", [128, D], F32) for j in range(2)]

                def phaseT2(h):
                    wd_uses = 0
                    dn_free = {}
                    g_free = {0: 0, 1: 0}
                    ob_uses = {0: 0, 1: 0}
                    xc_uses = {0: 0, 1: 0}
                    tm_uses = {0: 0, 1: 0}
                    for half in range(2):
                        h.W("gpsimd", "mm", wd_uses)
                        for i in range(6):
                            h.dma("gpsimd", wd_sb[i][:],
                                  bass.AP(dram["wd"].tensor,
                                          512 * 1024 * i + 512 * half,
                                          [[1024, 128], [128 * 1024, 4],
                                           [1, 512]]), "dwd")
                        h.W("tensor", "dwd")
                        for tt in range(16):
                            pg = tt % 4
                            dn = bass.AP(ps_cx if pg < 2 else ps_sc,
                                         1024 * (pg % 2),
                                         [[2048, 128], [1, 512]])
                            h.W("tensor", "ac", dn_free.get(pg, 0))
                            for hb in range(24):
                                h.mm(dn,
                                     h_sb[hb][:, 128 * tt:128 * (tt + 1)],
                                     wd_sb[hb // 4][:,
                                                    512 * (hb % 4):
                                                    512 * (hb % 4 + 1)],
                                     start=(hb == 0), stop=(hb == 23),
                                     inc=(hb == 23))
                            wd_uses = h.cnt("mm")
                            h.W("scalar", "mm")
                            if half == 0:
                                gdst = g_all[tt][:]
                            else:
                                h.W("scalar", "ev", g_free[tt % 2])
                                gdst = g_b[tt % 2][:]
                            h.act(gdst, dn, AF.Silu, scale=DQ)
                            dn_free[pg] = h.cnt("ac")
                            if half == 1:
                                pb = tt % 2
                                xsem = "dxc0" if pb == 0 else "dxc1"
                                tsem = "dtm0" if pb == 0 else "dtm1"
                                osem = "dou0" if pb == 0 else "dou1"
                                h.W("sync", "ev", xc_uses[pb])
                                h.dma("sync", x_cmb[pb][:],
                                      dram["x"][128 * tt:128 * (tt + 1), :],
                                      xsem)
                                h.W("sync", "ev", tm_uses[pb])
                                h.dma("sync", s_tm[pb][:],
                                      s_hbm[:, 128 * tt:128 * (tt + 1)], tsem,
                                      transpose=True)
                                h.W("vector", "ac")
                                h.W("vector", xsem)
                                h.W("vector", tsem)
                                h.W("vector", osem, 16 * ob_uses[pb])
                                for hh in range(H):
                                    gsrc = (g_all[tt][:, 64 * hh:
                                                      64 * (hh + 1)]
                                            if hh < 8 else
                                            g_b[pb][:, 64 * (hh - 8):
                                                    64 * (hh - 7)])
                                    h.V("scalar_tensor_tensor",
                                        out_b[pb][:, 64 * hh:64 * (hh + 1)],
                                        s_tm[pb][:, 64 * hh:64 * (hh + 1)],
                                        rec_tm[:, 16 * tt + hh:
                                               16 * tt + hh + 1],
                                        gsrc, AL.mult, AL.mult,
                                        inc=(hh == H - 1))
                                h.V("tensor_add", out_b[pb][:], out_b[pb][:],
                                    x_cmb[pb][:])
                                g_free[pb] = h.cnt("ev")
                                xc_uses[pb] = h.cnt("ev")
                                tm_uses[pb] = h.cnt("ev")
                                h.W("sync", "ev")
                                h.dma("sync", y[128 * tt:128 * (tt + 1), :],
                                      out_b[pb][:], osem)
                                ob_uses[pb] += 1
                    h.W("sync", "dou0")
                    h.W("sync", "dou1")

                phase(phaseT2)

    return nc


def _prep_inputs(L0_attn_x, L0_attn_Wq, L0_attn_bq, L0_attn_Wqc, L0_attn_bqc,
                 L0_attn_Wk, L0_attn_bk, L0_attn_Wkc, L0_attn_bkc,
                 L0_attn_Wv, L0_attn_bv, L0_attn_Wvc, L0_attn_bvc,
                 L0_ffn_w_gate, L0_ffn_w_up, L0_ffn_w_down):
    f32 = np.float32

    def pad_w(w):
        out = np.zeros((D, 512), f32)
        for hh in range(H):
            out[:, 32 * hh:32 * hh + 16] = w[:, 16 * hh:16 * (hh + 1)]
        return out

    def pad_b(b):
        out = np.zeros(512, f32)
        for hh in range(H):
            out[32 * hh:32 * hh + 16] = b[16 * hh:16 * (hh + 1)]
        return np.ascontiguousarray(out.reshape(4, 128).T)

    wg = np.asarray(L0_ffn_w_gate)
    wu = np.asarray(L0_ffn_w_up)
    wd = np.asarray(L0_ffn_w_down)
    common = {
        "wq": pad_w(np.asarray(L0_attn_Wq, f32)),
        "wqc": pad_w(np.asarray(L0_attn_Wqc, f32)),
        "wk": pad_w(np.asarray(L0_attn_Wk, f32)),
        "wkc": pad_w(np.asarray(L0_attn_Wkc, f32)),
        "wv": np.ascontiguousarray(np.asarray(L0_attn_Wv, f32)),
        "wvc": np.ascontiguousarray(np.asarray(L0_attn_Wvc, f32)),
        "bq4": pad_b(np.asarray(L0_attn_bq, f32)),
        "bqc4": pad_b(np.asarray(L0_attn_bqc, f32)),
        "bk4": pad_b(np.asarray(L0_attn_bk, f32)),
        "bkc4": pad_b(np.asarray(L0_attn_bkc, f32)),
        "bvr": np.ascontiguousarray(np.asarray(L0_attn_bv, f32).reshape(1, D)),
        "bvcr": np.ascontiguousarray(np.asarray(L0_attn_bvc, f32).reshape(1, D)),
        "wg": np.ascontiguousarray(
            wg.T.reshape(1024, 24, 128).transpose(1, 0, 2)).reshape(
                24 * 1024, 128),
        "wu": np.ascontiguousarray(
            wu.T.reshape(1024, 24, 128).transpose(1, 0, 2)).reshape(
                24 * 1024, 128),
        "wd": np.ascontiguousarray(wd.T),
        "ident": np.eye(16, dtype=f32),
        "bo4": np.kron(np.eye(4, dtype=f32), np.ones((32, 1), f32)),
    }
    x = np.asarray(L0_attn_x, f32)
    in_maps = []
    for core in range(8):
        b, qh = core // 2, core % 2
        xb = x[b]
        if qh == 1:
            xb = np.concatenate([xb[MY:], xb[:MY]], axis=0)
        m = dict(common)
        m["x"] = np.ascontiguousarray(xb)
        in_maps.append(m)
    return in_maps


def kernel(**inputs):
    if "nc" not in _CACHE:
        _CACHE["nc"] = _build()
    nc = _CACHE["nc"]
    in_maps = _prep_inputs(**inputs)
    res = run_bass_kernel_spmd(nc, in_maps, core_ids=list(range(8)))
    out = np.empty((4, NT, D), np.float32)
    for core in range(8):
        b, qh = core // 2, core % 2
        out[b, MY * qh:MY * (qh + 1)] = res.results[core]["y"]
    return out


# version 35 (final)
# speedup vs baseline: 1.0287x; 1.0287x over previous
import numpy as np
from contextlib import ExitStack

import concourse.bass as bass
import concourse.mybir as mybir
from concourse.bass_utils import run_bass_kernel_spmd

F32 = mybir.dt.float32
BF16 = mybir.dt.bfloat16
I32 = mybir.dt.int32
AF = mybir.ActivationFunctionType
AL = mybir.AluOpType
AX = mybir.AxisListType.X

NT, MY, D, H = 4096, 2048, 1024, 16
KC = 8
NKT = 33
EPS = 1e-5
DQ = 1.0 / 127.0

ENGINES = ["sync", "scalar", "tensor", "vector", "gpsimd"]
SEM_NAMES = [
    "mm", "ac", "ev",
    "dxa", "dxb", "dst", "dtl", "dwv", "dwq", "dwk", "dmsc",
    "dsp0", "dsp1", "dwd", "dgw", "dgw1", "ddn",
    "dpk0", "dpk1", "dgl0", "dgl1", "dut0", "dut1", "dht0", "dht1",
    "dhl0", "dhl1", "dxc0", "dxc1", "dtm0", "dtm1", "dou0", "dou1",
]

_CACHE = {}


class Hp:

    def __init__(self, me, eng, ct, st, sems):
        self.me, self.eng, self.ct, self.st, self.sems = me, eng, ct, st, sems

    def add(self, k, by=1):
        self.ct[k] = self.ct.get(k, 0) + by
        return self.ct[k]

    def cnt(self, k):
        return self.ct.get(k, 0)

    def W(self, owner, sem, val=None):
        if self.me == owner:
            self.eng.wait_ge(self.sems[sem], self.cnt(sem) if val is None else val)

    def dma(self, q, out, in_, sem, **kw):
        if self.me == q:
            self.eng.dma_start(out=out, in_=in_, **kw).then_inc(self.sems[sem], 16)
        self.add(sem, 16)

    def mm(self, out, lhsT, rhs, start=True, stop=True, tp=None, inc=True):
        if self.me == "tensor":
            i = self.eng.matmul(out, lhsT, rhs, start=start, stop=stop,
                                tile_position=tp, skip_group_check=True)
            if inc:
                i.then_inc(self.sems["mm"], 1)
        if inc:
            self.add("mm")

    def tr(self, out, in_, ident, inc=True):
        if self.me == "tensor":
            i = self.eng.transpose(out, in_, ident)
            if inc:
                i.then_inc(self.sems["mm"], 1)
        if inc:
            self.add("mm")

    def act(self, out, in_, func, bias=0.0, scale=1.0, inc=True):
        if self.me == "scalar":
            i = self.eng.activation(out, in_, func, bias=bias, scale=scale)
            if inc:
                i.then_inc(self.sems["ac"], 1)
        if inc:
            self.add("ac")

    def V(self, method, *args, inc=True, **kw):
        if self.me == "vector":
            i = getattr(self.eng, method)(*args, **kw)
            if inc:
                i.then_inc(self.sems["ev"], 1)
        if inc:
            self.add("ev")

    def barrier(self, dma_sems=()):
        for s in ("mm", "ac", "ev"):
            own = {"mm": "tensor", "ac": "scalar", "ev": "vector"}[s]
            if self.me != own:
                self.W(self.me, s)
        for s in dma_sems:
            self.W(self.me, s)


def _build():
    nc = bass.Bass("TRN2", target_bir_lowering=False,
                   detect_race_conditions=False)
    dram = {}

    def din(name, shape, dt):
        dram[name] = nc.dram_tensor(name, shape, dt, kind="ExternalInput").ap()

    din("x", [NT, D], F32)
    din("wq", [D, 512], F32); din("wqc", [D, 512], F32)
    din("wk", [D, 512], F32); din("wkc", [D, 512], F32)
    din("wv", [D, D], F32); din("wvc", [D, D], F32)
    din("bq4", [128, 4], F32); din("bqc4", [128, 4], F32)
    din("bk4", [128, 4], F32); din("bkc4", [128, 4], F32)
    din("bvr", [1, D], F32); din("bvcr", [1, D], F32)
    din("wg", [24 * 1024, 128], I32)
    din("wu", [24 * 1024, 128], I32)
    din("wd", [3072, 1024], I32)
    din("ident", [16, 16], F32)
    din("bo4", [128, 4], F32)
    y = nc.dram_tensor("y", [MY, D], F32, kind="ExternalOutput").ap()
    xn_hbm = nc.dram_tensor("xn_hbm", [NT, D], BF16, kind="Internal").ap()
    s_hbm = nc.dram_tensor("s_hbm", [D, MY], BF16, kind="Internal").ap()
    dn_hbm = nc.dram_tensor("dn_hbm", [16, MY], F32, kind="Internal").ap()
    gr_hbm = nc.dram_tensor("gr_hbm", [3072, MY], BF16, kind="Internal").ap()
    ur_hbm = nc.dram_tensor("ur_hbm", [3072, MY], BF16, kind="Internal").ap()
    h_hbm = nc.dram_tensor("h_hbm", [3072, MY], BF16, kind="Internal").ap()

    cts = {e: {} for e in ENGINES}
    sts = {e: {} for e in ENGINES}

    with ExitStack() as top:
        def sb(stk, name, shape, dt):
            return stk.enter_context(nc.sbuf_tensor(name, shape, dt))

        ones1 = sb(top, "ones1", [1, 128], BF16)
        ones16 = sb(top, "ones16", [128, 16], BF16)
        blockones = sb(top, "blockones", [128, 4], BF16)
        idb = sb(top, "idb", [16, 16], F32)
        eps_t = sb(top, "eps_t", [128, 1], F32)
        bq4 = sb(top, "sbq4", [128, 4], F32); bqc4 = sb(top, "sbqc4", [128, 4], F32)
        bk4 = sb(top, "sbk4", [128, 4], F32); bkc4 = sb(top, "sbkc4", [128, 4], F32)
        bvr_b = sb(top, "bvr_b", [1, D], BF16)
        bvcr_b = sb(top, "bvcr_b", [1, D], BF16)
        normbuf = sb(top, "normbuf", [4, 64], F32)
        nb_row = sb(top, "nb_row", [1, 200], F32)
        m_sc = sb(top, "m_sc", [1, 4], F32)
        mneg_b = sb(top, "mneg_b", [1, 1], BF16)
        bias_mneg = sb(top, "bias_mneg", [128, 1], F32)
        kp4 = sb(top, "kp4", [128, 4], F32)
        sig4 = sb(top, "sig4", [128, 4], F32)
        kp4b = sb(top, "kp4b", [128, 4], BF16)
        kp_t = [sb(top, f"kp_t{i}", [128, 128], BF16) for i in range(4)]
        dpiece = sb(top, "dpiece", [1, 1024], F32)
        xn_my = [sb(top, f"xn_my{c}", [128, MY], BF16) for c in range(KC)]

        ps_sc = top.enter_context(nc.psum_tensor("ps_sc", [128, 2048], F32))
        ps_cx = top.enter_context(nc.psum_tensor("ps_cx", [128, 2048], F32))
        sems = {n: top.enter_context(nc.semaphore(n)) for n in SEM_NAMES}

        def phase(fn):
            with nc.Block() as block:
                for ename in ENGINES:
                    def body(eng, ename=ename):
                        fn(Hp(ename, eng, cts[ename], sts[ename], sems))
                    getattr(block, ename)(body)

        with nc.Block() as b0:
            @b0.sync
            def _(sync):
                for n in SEM_NAMES:
                    sync.sem_clear(sems[n])

        with ExitStack() as stVQK:
            v_aug = [sb(stVQK, f"v_aug{t}", [128, 1040], BF16) for t in range(NKT)]
            q_sb = [sb(stVQK, f"q_sb{c}", [128, MY], BF16) for c in range(4)]
            k_pad = [sb(stVQK, f"k_pad{c}", [128, NT], BF16) for c in range(4)]

            with ExitStack() as stXN:
                xn_pr = [sb(stXN, f"xn_pr{c}", [128, MY], BF16) for c in range(KC)]

                def xn_chunk(c, col0, n):
                    if col0 < MY:
                        return xn_my[c][:, col0:col0 + n]
                    return xn_pr[c][:, col0 - MY:col0 - MY + n]

                with ExitStack() as stA:
                    xt = [sb(stA, f"xt{j}", [128, D], F32) for j in range(2)]
                    xnt = sb(stA, "xnt", [128, D], BF16)
                    xsq = sb(stA, "xsq", [128, D], F32)
                    ln = [sb(stA, f"ln{i}", [128, 32], F32) for i in range(4)]
                    (ln_s1, ln_s2, ln_mu, ln_rs) = ln

                    def phaseA(h):
                        h.V("memset", xsq[:], 1.0)
                        h.W("scalar", "ev")
                        for k in range(3):
                            h.act(xsq[:, 256 * k:256 * (k + 1)],
                                  xsq[:, 256 * k:256 * (k + 1)], AF.Sqrt)
                        h.V("memset", ones1[:], 1.0)
                        h.V("memset", eps_t[:], EPS)
                        h.V("memset", ones16[:], 1.0)
                        h.dma("gpsimd", blockones[:], dram["bo4"], "dmsc")
                        for nm, dst in [("bq4", bq4), ("bqc4", bqc4),
                                        ("bk4", bk4), ("bkc4", bkc4),
                                        ("ident", idb)]:
                            h.dma("gpsimd", dst[:], dram[nm], "dmsc")
                        h.dma("gpsimd", bvr_b[:], dram["bvr"], "dmsc")
                        h.dma("gpsimd", bvcr_b[:], dram["bvcr"], "dmsc")
                        xt_free = {0: 0, 1: 0}
                        ssem = ["dxa", "dxb"]
                        xq = ["sync", "gpsimd"]
                        for tt in range(NT // 128):
                            j = tt % 2
                            if tt >= 2:
                                h.W(xq[j], "ev", xt_free[j])
                            h.dma(xq[j], xt[j][:],
                                  dram["x"][128 * tt:128 * (tt + 1), :], ssem[j])
                            h.W("vector", ssem[j])
                            h.V("reduce_sum", ln_s1[:, tt:tt + 1], xt[j][:],
                                axis=AX)
                            h.V("tensor_mul", xsq[:], xt[j][:], xt[j][:])
                            h.V("reduce_sum", ln_s2[:, tt:tt + 1], xsq[:],
                                axis=AX)
                            xt_free[j] = h.cnt("ev")
                        h.V("tensor_scalar_mul", ln_mu[:], ln_s1[:], 1.0 / D)
                        h.V("tensor_scalar_mul", ln_s1[:], ln_s2[:], 1.0 / D)
                        h.V("tensor_mul", ln_s2[:], ln_mu[:], ln_mu[:])
                        h.V("tensor_sub", ln_s1[:], ln_s1[:], ln_s2[:])
                        h.W("scalar", "ev")
                        h.act(ln_s2[:], ln_s1[:], AF.Sqrt, bias=eps_t[:, 0:1])
                        h.W("vector", "ac")
                        h.V("reciprocal", ln_rs[:], ln_s2[:])
                        for tt in range(NT // 128):
                            j = tt % 2
                            h.W("gpsimd", "ev", xt_free[j])
                            h.dma("gpsimd", xt[j][:],
                                  dram["x"][128 * tt:128 * (tt + 1), :], ssem[j])
                            h.W("vector", ssem[j])
                            h.W("vector", "dst", 16 * tt)
                            h.V("tensor_scalar", xnt[:], xt[j][:],
                                ln_mu[:, tt:tt + 1], ln_rs[:, tt:tt + 1],
                                AL.subtract, AL.mult)
                            xt_free[j] = h.cnt("ev")
                            h.W("sync", "ev")
                            h.dma("sync", xn_hbm[128 * tt:128 * (tt + 1), :],
                                  xnt[:], "dst")
                            if tt == 15:
                                h.W("sync", "dst", 16 * 16)
                                for c in range(KC):
                                    h.dma("sync", xn_my[c][:],
                                          xn_hbm[0:MY, 128 * c:128 * (c + 1)],
                                          "dtl", transpose=True)
                        h.W("scalar", "ev")
                        h.act(xsq[:, 768:1024], xsq[:, 768:1024], AF.Sigmoid)
                        h.W("sync", "dst")
                        for c in range(KC):
                            h.dma("sync", xn_pr[c][:],
                                  xn_hbm[MY:NT, 128 * c:128 * (c + 1)], "dtl",
                                  transpose=True)
                        h.barrier(dma_sems=["dst", "dmsc"])

                    phase(phaseA)

                with ExitStack() as stB:
                    sigb = [sb(stB, f"sigb{j}", [128, 512], F32) for j in range(2)]

                    with ExitStack() as stWV:
                        wvb = sb(stWV, "wvb", [128, 2048], BF16)
                        wvcb = sb(stWV, "wvcb", [128, 2048], BF16)

                        def phaseB3(h):
                            h.st.setdefault("pfree", {0: 0, 1: 0})
                            h.st.setdefault("sfree", {0: 0, 1: 0})
                            h.st.setdefault("w_uses", 0)
                            for t in range(32):
                                h.V("memset",
                                    bass.AP(v_aug[t], 64, [[1040, 128], [65, 16]]),
                                    1.0, inc=(t == 31))
                            for qt in range(4):
                                h.W("gpsimd", "mm", h.st["w_uses"])
                                h.dma("gpsimd", wvb[:],
                                      bass.AP(dram["wv"].tensor, 256 * qt,
                                              [[1024, 128], [128 * 1024, 8],
                                               [1, 256]]), "dwv")
                                h.dma("gpsimd", wvcb[:],
                                      bass.AP(dram["wvc"].tensor, 256 * qt,
                                              [[1024, 128], [128 * 1024, 8],
                                               [1, 256]]), "dwv")
                                h.W("tensor", "dwv")
                                if qt == 0:
                                    h.W("tensor", "dtl", 16 * KC)
                                for tt in list(range(16)) + list(range(16, 32)):
                                    if qt == 0 and tt == 16:
                                        h.W("tensor", "dtl", 32 * KC)
                                    p = tt % 2
                                    raw = bass.AP(ps_sc, 1024 * p,
                                                  [[2048, 128], [1, 256]])
                                    gate = bass.AP(ps_sc, 1024 * p + 512,
                                                   [[2048, 128], [1, 256]])
                                    h.W("tensor", "ev", h.st["pfree"][p])
                                    h.mm(raw, ones1[0:1, 0:128],
                                         bvr_b[0:1, 256 * qt:256 * (qt + 1)],
                                         start=True, stop=False, inc=False)
                                    h.mm(gate, ones1[0:1, 0:128],
                                         bvcr_b[0:1, 256 * qt:256 * (qt + 1)],
                                         start=True, stop=False, inc=False)
                                    for kc in range(KC):
                                        xnc = xn_chunk(kc, 128 * tt, 128)
                                        h.mm(raw, xnc,
                                             wvb[:, 256 * kc:256 * (kc + 1)],
                                             start=False, stop=(kc == KC - 1),
                                             inc=False)
                                    for kc in range(KC):
                                        xnc = xn_chunk(kc, 128 * tt, 128)
                                        h.mm(gate, xnc,
                                             wvcb[:, 256 * kc:256 * (kc + 1)],
                                             start=False, stop=(kc == KC - 1),
                                             inc=(kc == KC - 1))
                                    h.st["w_uses"] = h.cnt("mm")
                                    h.W("scalar", "mm")
                                    h.W("scalar", "ev", h.st["sfree"][p])
                                    h.act(sigb[p][:, 0:256], gate, AF.Sigmoid)
                                    h.W("vector", "ac")
                                    h.V("tensor_mul",
                                        bass.AP(v_aug[tt], 260 * qt,
                                                [[1040, 128], [65, 4], [1, 64]]),
                                        bass.AP(ps_sc, 1024 * p,
                                                [[2048, 128], [64, 4], [1, 64]]),
                                        bass.AP(sigb[p], 0,
                                                [[512, 128], [64, 4], [1, 64]]))
                                    h.st["pfree"][p] = h.cnt("ev")
                                    h.st["sfree"][p] = h.cnt("ev")
                            h.barrier(dma_sems=["dwv"])

                        phase(phaseB3)

                    with ExitStack() as stN:
                        nsq = sb(stN, "nsq", [128, MY], BF16)

                        def mk_proj(dst, wname, wcname, b_ap, bc_ap, ntiles,
                                    norm_col0, slot_sem, wqb, wqcb):
                            def run(h):
                                pfree, sfree = h.st["pfree"], h.st["sfree"]
                                for cc in range(4):
                                    h.W("gpsimd", "mm", h.st["w_uses"])
                                    h.dma("gpsimd", wqb[:],
                                          bass.AP(dram[wname].tensor, 128 * cc,
                                                  [[512, 128], [128 * 512, 8],
                                                   [1, 128]]), slot_sem)
                                    h.dma("gpsimd", wqcb[:],
                                          bass.AP(dram[wcname].tensor, 128 * cc,
                                                  [[512, 128], [128 * 512, 8],
                                                   [1, 128]]), slot_sem)
                                    h.W("tensor", slot_sem)
                                    for nt in range(ntiles):
                                        p = nt % 2
                                        raw = bass.AP(ps_sc, 1024 * p,
                                                      [[2048, 128], [1, 512]])
                                        gate = bass.AP(ps_sc, 1024 * p + 512,
                                                       [[2048, 128], [1, 512]])
                                        h.W("tensor", "ev", pfree[p])
                                        for kc in range(KC):
                                            xnc = xn_chunk(kc, 512 * nt, 512)
                                            h.mm(raw,
                                                 wqb[:, 128 * kc:128 * (kc + 1)],
                                                 xnc, start=(kc == 0),
                                                 stop=(kc == KC - 1), inc=False)
                                        for kc in range(KC):
                                            xnc = xn_chunk(kc, 512 * nt, 512)
                                            h.mm(gate,
                                                 wqcb[:, 128 * kc:128 * (kc + 1)],
                                                 xnc, start=(kc == 0),
                                                 stop=(kc == KC - 1),
                                                 inc=(kc == KC - 1))
                                        h.st["w_uses"] = h.cnt("mm")
                                        h.W("scalar", "mm")
                                        h.W("scalar", "ev", sfree[p])
                                        h.act(sigb[p][:, 0:512], gate, AF.Sigmoid,
                                              bias=bc_ap[:, cc:cc + 1])
                                        h.W("vector", "ac")
                                        h.V("scalar_tensor_tensor",
                                            dst[cc][:, 512 * nt:512 * (nt + 1)],
                                            raw, b_ap[:, cc:cc + 1],
                                            sigb[p][:, 0:512], AL.add, AL.mult)
                                        pfree[p] = h.cnt("ev")
                                        sfree[p] = h.cnt("ev")
                                    for w0 in range(0, ntiles, 4):
                                        wn = min(4, ntiles - w0)
                                        h.V("tensor_mul", nsq[:, 0:512 * wn],
                                            dst[cc][:, 512 * w0:512 * (w0 + wn)],
                                            dst[cc][:, 512 * w0:512 * (w0 + wn)])
                                        h.W("tensor", "ev")
                                        for k2 in range(wn):
                                            pn = k2 % 2
                                            h.W("tensor", "ev", pfree[pn])
                                            h.mm(bass.AP(ps_sc, 1024 * pn,
                                                         [[2048, 4], [1, 512]]),
                                                 blockones[:, 0:4],
                                                 nsq[:, 512 * k2:512 * (k2 + 1)],
                                                 start=True, stop=True)
                                            h.W("vector", "mm")
                                            idx = norm_col0 + ntiles * cc + w0 + k2
                                            h.V("reduce_max",
                                                normbuf[0:4, idx:idx + 1],
                                                bass.AP(ps_sc, 1024 * pn,
                                                        [[2048, 4], [1, 512]]),
                                                axis=AX)
                                            pfree[pn] = h.cnt("ev")
                                h.barrier(dma_sems=[slot_sem])
                            return run

                        with ExitStack() as stWQ:
                            wqb = sb(stWQ, "wqb", [128, 1024], BF16)
                            wqcb = sb(stWQ, "wqcb", [128, 1024], BF16)
                            phase(mk_proj(q_sb, "wq", "wqc", bq4, bqc4, MY // 512,
                                          0, "dwq", wqb, wqcb))
                        with ExitStack() as stWK:
                            wkb = sb(stWK, "wkb", [128, 1024], BF16)
                            wkcb = sb(stWK, "wkcb", [128, 1024], BF16)
                            phase(mk_proj(k_pad, "wk", "wkc", bk4, bkc4, NT // 512,
                                          16, "dwk", wkb, wkcb))

            stB4 = ExitStack()
            vp_row = sb(stB4, "vp_row", [1, D], BF16)
            var_row = sb(stB4, "var_row", [1, 1040], BF16)

            def phaseB4(h):
                h.act(sig4[:], bkc4[:], AF.Sigmoid)
                h.act(vp_row[:], bvcr_b[:], AF.Sigmoid)
                h.W("vector", "ac")
                h.V("tensor_mul", kp4[:], bk4[:], sig4[:])
                h.V("tensor_mul", vp_row[:], bvr_b[:], vp_row[:])
                for qd in range(4):
                    h.V("memset", kp_t[qd][:], 0.0)
                    h.V("tensor_scalar_mul", kp_t[qd][:, 0:16], ones16[:],
                        kp4[:, qd:qd + 1])
                h.V("memset", v_aug[32][:], 0.0)
                h.V("memset", var_row[:], 0.0)
                h.V("tensor_copy",
                    bass.AP(var_row, 0, [[1040, 1], [65, 16], [1, 64]]),
                    bass.AP(vp_row, 0, [[1024, 1], [64, 16], [1, 64]]))
                h.V("memset", bass.AP(var_row, 64, [[1040, 1], [65, 16]]), 1.0)
                h.W("tensor", "ev")
                for i, (c0, n) in enumerate([(0, 512), (512, 512), (1024, 16)]):
                    h.mm(bass.AP(ps_cx, c0, [[2048, 16], [1, n]]),
                         ones1[0:1, 0:16], var_row[0:1, c0:c0 + n],
                         start=True, stop=True, inc=(i == 2))
                h.W("vector", "mm")
                h.V("tensor_copy", v_aug[32][0:16, :],
                    bass.AP(ps_cx, 0, [[2048, 16], [1, 1040]]))
                h.V("tensor_mul", kp4b[:], kp4[:], kp4[:])
                h.W("tensor", "ev")
                h.mm(bass.AP(ps_cx, 0, [[2048, 4], [1, 4]]),
                     blockones[:, 0:4], kp4b[:, 0:4], start=True, stop=True)
                h.W("vector", "mm")
                h.V("reduce_max", normbuf[0:4, 48:49],
                    bass.AP(ps_cx, 0, [[2048, 4], [1, 4]]), axis=AX)
                h.W("gpsimd", "ev")
                h.dma("gpsimd", nb_row[0:1, 0:196], normbuf[0:4, 0:49], "dmsc")
                h.W("vector", "dmsc")
                h.V("reduce_max", m_sc[0:1, 0:1], nb_row[0:1, 0:196], axis=AX)
                h.V("tensor_scalar_mul", mneg_b[:], m_sc[0:1, 0:1], -0.1275)
                h.W("tensor", "ev")
                h.mm(bass.AP(ps_cx, 0, [[2048, 128], [1, 1]]),
                     ones1[0:1, 0:128], mneg_b[0:1, 0:1],
                     start=True, stop=True)
                h.W("vector", "mm")
                h.V("tensor_copy", bias_mneg[:],
                    bass.AP(ps_cx, 0, [[2048, 128], [1, 1]]))
                h.act(sig4[:], sig4[:], AF.Exp, scale=0.0)
                h.barrier(dma_sems=["dmsc"])

            phase(phaseB4)
            stB4.close()

            with ExitStack() as stC:
                p_sb = [sb(stC, f"p_sb{j}", [128, 1024], BF16) for j in range(3)]
                s_piece = [sb(stC, f"s_piece{j}", [128, 512], BF16)
                           for j in range(2)]
                ptile = [sb(stC, f"ptile{j}", [128, 2048], BF16)
                         for j in range(2)]
                wgb = [sb(stC, f"wgb{j}", [128, 2048], BF16) for j in range(2)]
                wub = [sb(stC, f"wub{j}", [128, 2048], BF16) for j in range(2)]

                def phaseC(h):
                    h.V("memset", p_sb[0][:], 0.0)
                    h.W("scalar", "ev")
                    for k in range(2):
                        h.act(p_sb[0][:, 512 * k:512 * (k + 1)],
                              p_sb[0][:, 512 * k:512 * (k + 1)], AF.Exp,
                              scale=0.0)

                    wg_uses = {0: 0, 1: 0}
                    wsem = {0: "dgw", 1: "dgw1"}
                    pk_sem = {0: "dpk0", 1: "dpk1"}
                    pk_uses = {0: 0, 1: 0}
                    ffn_free = {0: 0, 1: 0}

                    def load_super(sbi):
                        if sbi >= 12:
                            return
                        wj = sbi % 2
                        h.W("gpsimd", "mm", wg_uses[wj])
                        off = 2 * 1024 * 128 * sbi
                        h.dma("gpsimd", wgb[wj][:],
                              bass.AP(dram["wg"].tensor, off,
                                      [[128, 128], [1024 * 128, 2],
                                       [128 * 128, 8], [1, 128]]), wsem[wj])
                        h.dma("gpsimd", wub[wj][:],
                              bass.AP(dram["wu"].tensor, off,
                                      [[128, 128], [1024 * 128, 2],
                                       [128 * 128, 8], [1, 128]]), wsem[wj])

                    load_super(0)
                    load_super(1)

                    def emit_burst(b):
                        hb = b // 8
                        sub = b % 8
                        kind = sub // 4
                        nt = sub % 4
                        sbi = hb // 2
                        wj = sbi % 2
                        if b == 16 * sbi:
                            h.W("tensor", wsem[wj])
                        bank = b % 2
                        ps = bass.AP(ps_cx, 1024 + 512 * bank,
                                     [[2048, 128], [1, 512]])
                        h.W("tensor", "ev", ffn_free[bank])
                        wt = wgb[wj] if kind == 0 else wub[wj]
                        for kc in range(KC):
                            h.mm(ps, wt[:, 1024 * (hb % 2) + 128 * kc:
                                        1024 * (hb % 2) + 128 * (kc + 1)],
                                 xn_my[kc][:, 512 * nt:512 * (nt + 1)],
                                 start=(kc == 0), stop=(kc == KC - 1),
                                 inc=(kc == KC - 1))
                        if b == 16 * sbi + 15:
                            wg_uses[wj] = h.cnt("mm")
                            load_super(sbi + 2)
                        pslot = (b // 4) % 2
                        h.W("vector", "mm")
                        if nt == 0:
                            h.W("vector", pk_sem[pslot], 16 * pk_uses[pslot])
                        h.V("tensor_copy",
                            ptile[pslot][:, 512 * nt:512 * (nt + 1)], ps)
                        ffn_free[bank] = h.cnt("ev")
                        if nt == 3:
                            h.W("gpsimd", "ev")
                            tgt = gr_hbm if kind == 0 else ur_hbm
                            h.dma("gpsimd", tgt[128 * hb:128 * (hb + 1), :],
                                  ptile[pslot][:], pk_sem[pslot])
                            pk_uses[pslot] += 1

                    FILL_KTS = (1, 2, 14, 20, 26, 31)
                    burst = {"b": 0}
                    ev_ctx_free = h.cnt("ev")
                    slot_last = {0: 0, 1: 0}
                    ac_of = {}
                    unit = 0
                    for qd in range(4):
                        for pj in range(2):
                            jA, jB = 2 * pj, 2 * pj + 1
                            hA, hB = 4 * qd + jA, 4 * qd + jB
                            for qc in range(4):
                                pc = unit % 2
                                psem = "dsp0" if pc == 0 else "dsp1"
                                for kt in range(NKT + 2):
                                    par = kt % 2
                                    if kt < NKT:
                                        h.W("tensor", "ac", slot_last[par])
                                        for hx, j in ((0, jA), (1, jB)):
                                            if kt < 32:
                                                lhsT = k_pad[qd][
                                                    32 * j:32 * j + 16,
                                                    128 * kt:128 * (kt + 1)]
                                            else:
                                                lhsT = kp_t[qd][
                                                    32 * j:32 * j + 16, :]
                                            h.mm(
                                                ps_sc[:, 1024 * par + 512 * hx:
                                                      1024 * par +
                                                      512 * (hx + 1)],
                                                lhsT,
                                                q_sb[qd][32 * j:32 * j + 16,
                                                         512 * qc:
                                                         512 * (qc + 1)],
                                                start=True, stop=True,
                                                tp=(32 * j, 0), inc=(hx == 1))
                                        h.W("scalar", "mm")
                                        h.act(
                                            bass.AP(p_sb[kt % 3], 0,
                                                    [[1024, 128], [1, 1024]]),
                                            bass.AP(ps_sc, 1024 * par,
                                                    [[2048, 128], [1, 1024]]),
                                            AF.Exp, bias=bias_mneg[:, 0:1],
                                            scale=0.125)
                                        slot_last[par] = h.cnt("ac")
                                        ac_of[kt] = h.cnt("ac")
                                    if kt > 1:
                                        ks = kt - 2
                                        h.W("tensor", "ac", ac_of[ks])
                                        if ks == 0:
                                            h.W("tensor", "ev", ev_ctx_free)
                                        for hx, hh in ((0, hA), (1, hB)):
                                            h.mm(
                                                ps_cx[0:65,
                                                      512 * hx:512 * (hx + 1)],
                                                v_aug[ks][:, 65 * hh:
                                                          65 * hh + 65],
                                                p_sb[ks % 3][:,
                                                             512 * hx:
                                                             512 * (hx + 1)],
                                                start=(ks == 0),
                                                stop=(ks == 32),
                                                inc=(hx == 1))
                                    if kt in FILL_KTS and burst["b"] < 192:
                                        emit_burst(burst["b"])
                                        burst["b"] += 1
                                h.W("vector", "mm")
                                h.W("vector", psem, 16 * (unit // 2))
                                h.W("vector", "ddn", 16 * unit)
                                for hx in range(2):
                                    h.V("tensor_copy",
                                        s_piece[pc][64 * hx:64 * (hx + 1), :],
                                        ps_cx[0:64, 512 * hx:512 * (hx + 1)],
                                        inc=False)
                                h.V("tensor_copy", dpiece[0:1, 0:1024],
                                    ps_cx[64:65, 0:1024])
                                ev_ctx_free = h.cnt("ev")
                                h.W("sync", "ev")
                                h.dma("sync",
                                      s_hbm[128 * (2 * qd + pj):
                                            128 * (2 * qd + pj + 1),
                                            512 * qc:512 * (qc + 1)],
                                      s_piece[pc][:], psem)
                                h.dma("sync",
                                      dn_hbm[hA:hA + 2,
                                             512 * qc:512 * (qc + 1)],
                                      dpiece[0:1, 0:1024], "ddn")
                                unit += 1
                    while burst["b"] < 192:
                        emit_burst(burst["b"])
                        burst["b"] += 1
                    h.barrier(dma_sems=["dsp0", "dsp1", "ddn", "dpk0", "dpk1",
                                        "dgw", "dgw1"])

                phase(phaseC)

        with ExitStack() as stT:
            rec_tm = sb(stT, "rec_tm", [128, 256], F32)
            h_sb = [sb(stT, f"h_sb{i}", [128, MY], BF16) for i in range(24)]

            with ExitStack() as stT0:
                denom_sb = sb(stT0, "denom_sb", [16, MY], F32)
                recip = sb(stT0, "recip", [16, MY], F32)

                def phaseT0(h):
                    h.V("memset", rec_tm[0:16, 0:64], 0.0)
                    h.W("scalar", "ev")
                    for k in range(2):
                        h.act(rec_tm[0:16, 32 * k:32 * (k + 1)],
                              rec_tm[0:16, 32 * k:32 * (k + 1)],
                              AF.Silu, scale=0.0)
                    h.dma("gpsimd", denom_sb[:], dn_hbm, "dmsc")
                    h.W("vector", "dmsc")
                    h.V("reciprocal", recip[:], denom_sb[:])
                    h.W("tensor", "ev")
                    h.W("tensor", "ac")
                    tr_free = {0: 0, 1: 0}
                    for tt in range(16):
                        p = tt % 2
                        h.W("tensor", "ev", tr_free[p])
                        h.tr(bass.AP(ps_sc, 1024 * p, [[2048, 128], [1, 16]]),
                             recip[0:16, 128 * tt:128 * (tt + 1)], idb[:])
                        h.W("vector", "mm")
                        h.V("tensor_copy", rec_tm[:, 16 * tt:16 * (tt + 1)],
                            bass.AP(ps_sc, 1024 * p, [[2048, 128], [1, 16]]))
                        tr_free[p] = h.cnt("ev")
                    h.barrier(dma_sems=["dmsc"])

                phase(phaseT0)

            with ExitStack() as stT1:
                gtile = [sb(stT1, f"gtile{j}", [128, 2048], BF16)
                         for j in range(2)]
                utile = [sb(stT1, f"utile{j}", [128, 2048], BF16)
                         for j in range(2)]
                silu_t = [sb(stT1, f"silu_t{j}", [128, 2048], BF16)
                          for j in range(2)]

                def phaseT1(h):
                    gl_sem = {0: "dgl0", 1: "dgl1"}
                    ut_sem = {0: "dut0", 1: "dut1"}
                    sl_ac = {0: 0, 1: 0}
                    sl_ev = {0: 0, 1: 0}
                    for hb in range(24):
                        sl = hb % 2
                        if hb >= 2:
                            h.W("gpsimd", "ac", sl_ac[sl])
                            h.W("sync", "ev", sl_ev[sl])
                        h.dma("gpsimd", gtile[sl][:],
                              gr_hbm[128 * hb:128 * (hb + 1), :], gl_sem[sl])
                        h.dma("sync", utile[sl][:],
                              ur_hbm[128 * hb:128 * (hb + 1), :], ut_sem[sl])
                        h.W("scalar", gl_sem[sl])
                        h.W("scalar", "ev", sl_ev[sl])
                        h.act(silu_t[sl][:], gtile[sl][:], AF.Silu, scale=DQ)
                        sl_ac[sl] = h.cnt("ac")
                        h.W("vector", "ac")
                        h.W("vector", ut_sem[sl])
                        h.V("scalar_tensor_tensor", h_sb[hb][:], utile[sl][:],
                            DQ, silu_t[sl][:], AL.mult, AL.mult)
                        sl_ev[sl] = h.cnt("ev")
                    h.barrier()

                phase(phaseT1)

            with ExitStack() as stT2:
                wd_sb = [sb(stT2, f"wd_sb{i}", [128, 2048], BF16)
                         for i in range(6)]
                g_all = [sb(stT2, f"g_all{i}", [128, 512], BF16)
                         for i in range(16)]
                g_b = [sb(stT2, f"g_b{j}", [128, 512], F32) for j in range(2)]
                x_cmb = [sb(stT2, f"x_cmb{j}", [128, D], F32) for j in range(2)]
                s_tm = [sb(stT2, f"s_tm{j}", [128, D], BF16) for j in range(2)]
                out_b = [sb(stT2, f"out_b{j}", [128, D], F32) for j in range(2)]

                def phaseT2(h):
                    wd_uses = 0
                    dn_free = {}
                    g_free = {0: 0, 1: 0}
                    ob_uses = {0: 0, 1: 0}
                    xc_uses = {0: 0, 1: 0}
                    tm_uses = {0: 0, 1: 0}
                    for half in range(2):
                        h.W("gpsimd", "mm", wd_uses)
                        for i in range(6):
                            h.dma("gpsimd", wd_sb[i][:],
                                  bass.AP(dram["wd"].tensor,
                                          512 * 1024 * i + 512 * half,
                                          [[1024, 128], [128 * 1024, 4],
                                           [1, 512]]), "dwd")
                        h.W("tensor", "dwd")
                        for tt in range(16):
                            pg = tt % 4
                            dn = bass.AP(ps_cx if pg < 2 else ps_sc,
                                         1024 * (pg % 2),
                                         [[2048, 128], [1, 512]])
                            h.W("tensor", "ac", dn_free.get(pg, 0))
                            for hb in range(24):
                                h.mm(dn,
                                     h_sb[hb][:, 128 * tt:128 * (tt + 1)],
                                     wd_sb[hb // 4][:,
                                                    512 * (hb % 4):
                                                    512 * (hb % 4 + 1)],
                                     start=(hb == 0), stop=(hb == 23),
                                     inc=(hb == 23))
                            wd_uses = h.cnt("mm")
                            h.W("scalar", "mm")
                            if half == 0:
                                gdst = g_all[tt][:]
                            else:
                                h.W("scalar", "ev", g_free[tt % 2])
                                gdst = g_b[tt % 2][:]
                            h.act(gdst, dn, AF.Silu, scale=DQ)
                            dn_free[pg] = h.cnt("ac")
                            if half == 1:
                                pb = tt % 2
                                xsem = "dxc0" if pb == 0 else "dxc1"
                                tsem = "dtm0" if pb == 0 else "dtm1"
                                osem = "dou0" if pb == 0 else "dou1"
                                h.W("sync", "ev", xc_uses[pb])
                                h.dma("sync", x_cmb[pb][:],
                                      dram["x"][128 * tt:128 * (tt + 1), :],
                                      xsem)
                                h.W("sync", "ev", tm_uses[pb])
                                h.dma("sync", s_tm[pb][:],
                                      s_hbm[:, 128 * tt:128 * (tt + 1)], tsem,
                                      transpose=True)
                                h.W("vector", "ac")
                                h.W("vector", xsem)
                                h.W("vector", tsem)
                                h.W("vector", osem, 16 * ob_uses[pb])
                                for hh in range(H):
                                    gsrc = (g_all[tt][:, 64 * hh:
                                                      64 * (hh + 1)]
                                            if hh < 8 else
                                            g_b[pb][:, 64 * (hh - 8):
                                                    64 * (hh - 7)])
                                    h.V("scalar_tensor_tensor",
                                        out_b[pb][:, 64 * hh:64 * (hh + 1)],
                                        s_tm[pb][:, 64 * hh:64 * (hh + 1)],
                                        rec_tm[:, 16 * tt + hh:
                                               16 * tt + hh + 1],
                                        gsrc, AL.mult, AL.mult,
                                        inc=(hh == H - 1))
                                h.V("tensor_add", out_b[pb][:], out_b[pb][:],
                                    x_cmb[pb][:])
                                g_free[pb] = h.cnt("ev")
                                xc_uses[pb] = h.cnt("ev")
                                tm_uses[pb] = h.cnt("ev")
                                h.W("sync", "ev")
                                h.dma("sync", y[128 * tt:128 * (tt + 1), :],
                                      out_b[pb][:], osem)
                                ob_uses[pb] += 1
                    h.W("sync", "dou0")
                    h.W("sync", "dou1")

                phase(phaseT2)

    return nc


def _prep_inputs(L0_attn_x, L0_attn_Wq, L0_attn_bq, L0_attn_Wqc, L0_attn_bqc,
                 L0_attn_Wk, L0_attn_bk, L0_attn_Wkc, L0_attn_bkc,
                 L0_attn_Wv, L0_attn_bv, L0_attn_Wvc, L0_attn_bvc,
                 L0_ffn_w_gate, L0_ffn_w_up, L0_ffn_w_down):
    f32 = np.float32

    def pad_w(w):
        out = np.zeros((D, 512), f32)
        for hh in range(H):
            out[:, 32 * hh:32 * hh + 16] = w[:, 16 * hh:16 * (hh + 1)]
        return out

    def pad_b(b):
        out = np.zeros(512, f32)
        for hh in range(H):
            out[32 * hh:32 * hh + 16] = b[16 * hh:16 * (hh + 1)]
        return np.ascontiguousarray(out.reshape(4, 128).T)

    wg = np.asarray(L0_ffn_w_gate)
    wu = np.asarray(L0_ffn_w_up)
    wd = np.asarray(L0_ffn_w_down)
    common = {
        "wq": pad_w(np.asarray(L0_attn_Wq, f32)),
        "wqc": pad_w(np.asarray(L0_attn_Wqc, f32)),
        "wk": pad_w(np.asarray(L0_attn_Wk, f32)),
        "wkc": pad_w(np.asarray(L0_attn_Wkc, f32)),
        "wv": np.ascontiguousarray(np.asarray(L0_attn_Wv, f32)),
        "wvc": np.ascontiguousarray(np.asarray(L0_attn_Wvc, f32)),
        "bq4": pad_b(np.asarray(L0_attn_bq, f32)),
        "bqc4": pad_b(np.asarray(L0_attn_bqc, f32)),
        "bk4": pad_b(np.asarray(L0_attn_bk, f32)),
        "bkc4": pad_b(np.asarray(L0_attn_bkc, f32)),
        "bvr": np.ascontiguousarray(np.asarray(L0_attn_bv, f32).reshape(1, D)),
        "bvcr": np.ascontiguousarray(np.asarray(L0_attn_bvc, f32).reshape(1, D)),
        "wg": np.ascontiguousarray(
            wg.T.reshape(1024, 24, 128).transpose(1, 0, 2)).reshape(
                24 * 1024, 128),
        "wu": np.ascontiguousarray(
            wu.T.reshape(1024, 24, 128).transpose(1, 0, 2)).reshape(
                24 * 1024, 128),
        "wd": np.ascontiguousarray(wd.T),
        "ident": np.eye(16, dtype=f32),
        "bo4": np.kron(np.eye(4, dtype=f32), np.ones((32, 1), f32)),
    }
    x = np.asarray(L0_attn_x, f32)
    in_maps = []
    for core in range(8):
        b, qh = core // 2, core % 2
        xb = x[b]
        if qh == 1:
            xb = np.concatenate([xb[MY:], xb[:MY]], axis=0)
        m = dict(common)
        m["x"] = np.ascontiguousarray(xb)
        in_maps.append(m)
    return in_maps


def kernel(**inputs):
    if "nc" not in _CACHE:
        _CACHE["nc"] = _build()
    nc = _CACHE["nc"]
    in_maps = _prep_inputs(**inputs)
    res = run_bass_kernel_spmd(nc, in_maps, core_ids=list(range(8)))
    out = np.empty((4, NT, D), np.float32)
    for core in range(8):
        b, qh = core // 2, core % 2
        out[b, MY * qh:MY * (qh + 1)] = res.results[core]["y"]
    return out
